# version 02 (final)
# baseline (speedup 1.0000x reference)
import contextlib
import numpy as np
import ml_dtypes
import concourse.bass as bass
import concourse.mybir as mybir
from concourse.bass_utils import run_bass_kernel_spmd

F32 = mybir.dt.float32
BF16 = mybir.dt.bfloat16
I32 = mybir.dt.int32
AF = mybir.ActivationFunctionType
ALU = mybir.AluOpType
AX = mybir.AxisListType

T = 2048
D = 1024
H = 16
NCH = 8
TCN = 16
INNER = 4096
EPS = 1e-5
NEG = -1.0e30
QB = 2
NGRP = TCN // QB
STAGE = 4
DBG = []
FFN_ONLY = False
SKIP_ATTN = False
SKIP_KT2 = False
ATTN_PH = 4
ATTN_NOEXP = False
ATTN_STKT = False


class Sem:
    def __init__(self, h):
        self.h = h
        self.v = 0

    def add(self, n):
        self.v += n
        return self.v


class K:
    ENGS = ("t", "v", "s", "g", "y")

    def __init__(self):
        self.q = {e: [] for e in self.ENGS}

    def emit(self, eng, fn):
        self.q[eng].append(fn)

    def wait(self, eng, sem, val=None):
        w = sem.v if val is None else val
        if w > 0:
            self.emit(eng, lambda e, s=sem.h, w=w: e.wait_ge(s, w))

    def dma(self, eng, out, in_, sem=None, **kw):
        if sem is not None:
            sem.add(16)
            self.emit(eng, lambda e, o=out, i=in_, s=sem.h, kw=kw:
                      e.dma_start(out=o, in_=i, **kw).then_inc(s, 16))
        else:
            self.emit(eng, lambda e, o=out, i=in_, kw=kw:
                      e.dma_start(out=o, in_=i, **kw))

    def op(self, eng, name, sem=None, n=1, **kw):
        if sem is not None:
            sem.add(n)
            def fn(e, name=name, kw=kw, s=sem.h, n=n):
                getattr(e, name)(**kw).then_inc(s, n)
        else:
            def fn(e, name=name, kw=kw):
                getattr(e, name)(**kw)
        self.emit(eng, fn)


def l1_rope_tables():
    dk = 64
    inv_freq = (1.0 / (10000.0 ** (np.arange(0, dk, 2, dtype=np.float32) / dk))
                ).astype(np.float32)
    pos = np.arange(T, dtype=np.float32)
    theta = pos[:, None] * inv_freq[None, :]
    cos = np.cos(theta).astype(np.float32)
    sin = np.sin(theta).astype(np.float32)
    CC = np.zeros((128, T), np.float32)
    SS2 = np.zeros((128, T), np.float32)
    for p in range(128):
        j = (p % dk) // 2
        CC[p, :] = cos[:, j]
        SS2[p, :] = (-sin[:, j]) if (p % 2 == 0) else sin[:, j]
    return (np.ascontiguousarray(CC.astype(ml_dtypes.bfloat16)),
            np.ascontiguousarray(SS2.astype(ml_dtypes.bfloat16)))


def build_nc():
    nc = bass.Bass("TRN2", target_bir_lowering=False,
                   detect_race_conditions=False)
    k = K()

    hx = nc.dram_tensor("x_in", [T, D], F32, kind="ExternalInput").ap()
    hout = nc.dram_tensor("out", [T, D], F32, kind="ExternalOutput").ap()

    def win(name, shape, dt=F32):
        return nc.dram_tensor(name, shape, dt, kind="ExternalInput").ap()

    hw = {}
    for nm, sh in [("L0_attn_wq", [D, D]), ("L0_attn_wk", [D, D]),
                   ("L0_attn_wv", [D, D]), ("L0_attn_wo", [D, D]),
                   ("L0_attn_bq", [D]), ("L0_attn_bk", [D]),
                   ("L0_attn_bv", [D]), ("L0_attn_bo", [D]),
                   ("L0_attn_rot_cos", [4096, 32]),
                   ("L0_attn_rot_sin", [4096, 32]),
                   ("L0_ffn_w_gate", [INNER, D]), ("L0_ffn_w_up", [INNER, D]),
                   ("L0_ffn_w_down", [D, INNER]),
                   ("L1_attn_Wq", [D, D]), ("L1_attn_Wk", [D, D]),
                   ("L1_attn_Wv", [D, D]), ("L1_attn_Wo", [D, D]),
                   ("L1_ffn_weight_scale", [1]), ("L1_ffn_bias", [D])]:
        hw[nm] = win(nm, sh)
    hw["L1_ffn_packed_weight"] = win("L1_ffn_packed_weight", [262144], I32)

    s_w0 = {nm: nc.dram_tensor(f"scr_{nm}", [D, D], BF16, kind="Internal").ap()
            for nm in ("wq", "wk", "wv", "wo")}
    s_row = nc.dram_tensor("scr_row", [T], F32, kind="Internal").ap()
    s_x = nc.dram_tensor("scr_x", [D, T], F32, kind="Internal").ap()
    s_tg = nc.dram_tensor("scr_tg", [D, INNER], BF16, kind="Internal").ap()
    s_tu = nc.dram_tensor("scr_tu", [D, INNER], BF16, kind="Internal").ap()
    s_td = nc.dram_tensor("scr_td", [INNER, D], BF16, kind="Internal").ap()

    ident_bf = nc.inline_tensor(np.eye(128, dtype=ml_dtypes.bfloat16),
                                name="c_ib")
    ident_f = nc.inline_tensor(np.eye(128, dtype=np.float32), name="c_if")
    ones128_bf = nc.inline_tensor(np.ones((128, 128), dtype=ml_dtypes.bfloat16),
                                  name="c_o128")
    onesrow_bf = nc.inline_tensor(np.ones((1, 512), dtype=ml_dtypes.bfloat16),
                                  name="c_or")
    onesrow_f = nc.inline_tensor(np.ones((1, 128), dtype=np.float32),
                                 name="c_orf")
    sw_np = np.zeros((128, 128), np.float32)
    for j in range(64):
        sw_np[2 * j, 2 * j + 1] = 1.0
        sw_np[2 * j + 1, 2 * j] = 1.0
    swap_bf = nc.inline_tensor(sw_np.astype(ml_dtypes.bfloat16), name="c_sw")
    tri_neg_np = np.where(np.arange(128)[None, :] > np.arange(128)[:, None],
                          np.float32(NEG), np.float32(0.0)).astype(np.float32)
    tri_neg_c = nc.inline_tensor(tri_neg_np, name="c_trn")
    tri01_np = (np.arange(128)[:, None] <= np.arange(128)[None, :])
    tri01_c = nc.inline_tensor(tri01_np.astype(ml_dtypes.bfloat16),
                               name="c_tri")
    altsign_c = nc.inline_tensor(
        np.where(np.arange(128) % 2 == 0, 1.0, -1.0).astype(np.float32)
        .reshape(128, 1), name="c_alt")
    CC_np, SS2_np = l1_rope_tables()
    cc_c = nc.inline_tensor(CC_np, name="c_cc")
    ss2_c = nc.inline_tensor(SS2_np, name="c_ss2")
    cp_np = np.zeros((128, 14), np.float32)
    for i, vv in enumerate([EPS, 0.125, 1.0 / D, 1e-5, 127.0, -0.5, 0.5,
                            1.0, -1.0, 0.0, -0.125, 0.0, 12.5, -12.5]):
        cp_np[:, i] = vv
    cp_c = nc.inline_tensor(cp_np, name="c_cp")
    hsw_np = np.zeros((128, 128), np.float32)
    for i in range(128):
        hsw_np[i, i ^ 64] = 1.0
    hsw_c = nc.inline_tensor(hsw_np.astype(ml_dtypes.bfloat16), name="c_hsw")
    morow_c = nc.inline_tensor(
        np.full((1, 128), -0.125, dtype=np.float32), name="c_mor")
    msel_np = np.zeros((32, 128), np.float32)
    for p in range(128):
        msel_np[(p % 64) // 2, p] = 1.0
    msel_c = nc.inline_tensor(msel_np.astype(ml_dtypes.bfloat16),
                              name="c_msel")

    es = contextlib.ExitStack()

    def SBx(stack, name, shape, dt):
        return stack.enter_context(nc.sbuf_tensor(name, shape, dt))

    with es:
        x = [None] * NCH
        xgen = [0]

        def x_alloc(stack):
            g = xgen[0]
            xgen[0] += 1
            for c in range(NCH):
                x[c] = SBx(stack, f"x{c}g{g}", [128, T], F32)
        xn = [SBx(es, f"xn{c}", [128, T], BF16) for c in range(NCH)]
        cp = SBx(es, "cp", [128, 14], F32)
        ib = SBx(es, "ib", [128, 128], BF16)
        iff = SBx(es, "iff", [128, 128], F32)
        o128 = SBx(es, "o128", [128, 128], BF16)
        orow = SBx(es, "orow", [1, 512], BF16)
        orf = SBx(es, "orf", [1, 128], F32)
        mor = SBx(es, "mor", [1, 128], F32)
        hswp = SBx(es, "hswp", [128, 128], BF16)
        swb = SBx(es, "swb", [128, 128], BF16)
        trn = SBx(es, "trn", [128, 128], F32)
        tri = SBx(es, "tri", [128, 128], BF16)
        alt = SBx(es, "alt", [128, 1], F32)
        rowm = SBx(es, "rowm", [1, 128], F32)
        colA = SBx(es, "colA", [128, 8], F32)
        colC = SBx(es, "colC", [128, 8], F32)
        mcolb = SBx(es, "mcolb", [128, 16], BF16)
        cbias = SBx(es, "cbias", [128, 2], F32)
        rsrow = SBx(es, "rsrow", [1, 512], BF16)
        rbc = SBx(es, "rbc", [64, 512], BF16)
        bQ = SBx(es, "bQ", [128, 8], F32)
        bK = SBx(es, "bK", [128, 8], F32)
        bV = SBx(es, "bV", [128, 8], F32)
        bO = SBx(es, "bO", [128, 8], F32)
        csr = SBx(es, "csr", [128, 4], F32)
        swcol = SBx(es, "swcol", [128, 4], F32)
        acol = SBx(es, "acol", [128, TCN], F32)
        mselT = SBx(es, "mselT", [32, 128], BF16)

        ps = es.enter_context(nc.psum_tensor("ps", [128, 4096], F32))

        def bank(i, n=1):
            return ps[:, 512 * i:512 * (i + n)]

        sm_d = Sem(es.enter_context(nc.semaphore("sm_d")))
        sm_g = Sem(es.enter_context(nc.semaphore("sm_g")))
        sm_t = Sem(es.enter_context(nc.semaphore("sm_t")))
        sm_v = Sem(es.enter_context(nc.semaphore("sm_v")))
        sm_s = Sem(es.enter_context(nc.semaphore("sm_s")))
        sm_st = Sem(es.enter_context(nc.semaphore("sm_st")))

        for i, vv in enumerate([EPS, 0.125, 1.0 / D, 1e-5, 127.0, -0.5,
                                0.5, 1.0, -1.0, 0.0, -0.125]):
            try:
                nc.const_aps.register(F32, vv, cp[:, i:i + 1])
            except Exception:
                pass

        blk = es.enter_context(nc.Block())

        def vop(name, sem=None, **kw):
            k.op("v", name, sem=sem, **kw)

        def sop(name, sem=None, **kw):
            k.op("s", name, sem=sem, **kw)

        def top(name, sem=None, **kw):
            k.op("t", name, sem=sem, **kw)

        def mm(out, lhsT, rhs, start=True, stop=True, sem=None):
            k.op("t", "matmul", sem=sem, out=out, lhsT=lhsT, rhs=rhs,
                 start=start, stop=stop)

        def vpad(sem=None):
            vop("tensor_copy", sem=sem, out=xn[0][:, 0:1024],
                in_=xn[0][:, 0:1024])

        def barrier():
            for eng in ("t", "v", "s", "g", "y"):
                for sem in (sm_t, sm_v, sm_s, sm_g, sm_d):
                    k.wait(eng, sem)

        dbg_tensors = {}

        def dump(point, bufs):
            if point not in DBG:
                return
            barrier()
            for i, b in enumerate(bufs):
                shape = list(b.shape)
                dt = b.dtype
                h = nc.dram_tensor(f"dbg_{point}_{i}", shape, dt,
                                   kind="ExternalOutput").ap()
                dbg_tensors[f"dbg_{point}_{i}"] = True
                k.dma("y", h, b, sm_st)
            barrier()

        def dump_dram(point, src_ap, shape, dt):
            if point not in DBG:
                return
            barrier()
            h = nc.dram_tensor(f"dbg_{point}_0", shape, dt,
                               kind="ExternalOutput").ap()
            k.dma("y", h, src_ap, sm_st)
            barrier()

        vop("memset", ap=ps[:, :], constant=0.0)
        for z_ap in (mcolb[:], rowm[:], colA[:], colC[:], swcol[:],
                     acol[:], cbias[:]):
            vop("memset", ap=z_ap, constant=0.0)
        vop("memset", sem=sm_v, ap=rsrow[:], constant=0.0)
        k.wait("t", sm_v)
        k.wait("s", sm_v)
        for dst, src in ((cp, cp_c), (ib, ident_bf), (iff, ident_f),
                         (o128, ones128_bf), (orow, onesrow_bf),
                         (swb, swap_bf), (trn, tri_neg_c), (tri, tri01_c),
                         (alt, altsign_c), (mselT, msel_c), (orf, onesrow_f),
                         (mor, morow_c), (hswp, hsw_c)):
            k.dma("y", dst[:], src.ap(), sm_d)
        k.wait("t", sm_d)

        scx = contextlib.ExitStack()
        x_alloc(scx)
        with contextlib.ExitStack() as sc:
            stg = [SBx(sc, "ldst0", [128, T], F32),
                   SBx(sc, "ldst1", [128, T], F32)]
            evac_marks = []
            for pair in range(TCN // 2):
                sb_ = stg[pair % 2]
                if pair >= 2:
                    k.wait("y", sm_v, evac_marks[pair - 2])
                k.dma("y", sb_[:, 0:1024],
                      hx[(2 * pair) * 128:(2 * pair + 1) * 128, :], sm_d)
                k.dma("y", sb_[:, 1024:2048],
                      hx[(2 * pair + 1) * 128:(2 * pair + 2) * 128, :], sm_d)
                k.wait("t", sm_d)
                if pair >= 1:
                    k.wait("t", sm_v, evac_marks[pair - 1])
                for half in range(2):
                    for c in range(NCH):
                        off = (half * 4 + (c // 2)) * 512 + (c % 2) * 128
                        top("transpose",
                            sem=(sm_t if (half == 1 and c == NCH - 1) else None),
                            out=ps[:, off:off + 128],
                            in_=sb_[:, half * 1024 + c * 128:
                                    half * 1024 + c * 128 + 128],
                            identity=iff[:])
                k.wait("v", sm_t)
                for half in range(2):
                    tc = 2 * pair + half
                    for c in range(NCH):
                        off = (half * 4 + (c // 2)) * 512 + (c % 2) * 128
                        vop("tensor_copy",
                            sem=(sm_v if (half == 1 and c == NCH - 1) else None),
                            out=x[c][:, tc * 128:(tc + 1) * 128],
                            in_=ps[:, off:off + 128])
                evac_marks.append(sm_v.v)
            barrier()
        dump("xload", [b[:] for b in x])

        csrc = hw["L0_attn_rot_cos"].rearrange("a b -> (a b)")
        ssrc = hw["L0_attn_rot_sin"].rearrange("a b -> (a b)")
        k.dma("y", colA[0:32, 0:1],
              csrc[T * 32:T * 32 + 32].rearrange("(p o) -> p o", o=1), sm_d)
        k.dma("y", colA[0:32, 1:2],
              ssrc[T * 32:T * 32 + 32].rearrange("(p o) -> p o", o=1), sm_d)
        k.wait("v", sm_d)
        vop("tensor_copy", sem=sm_v, out=mcolb[0:32, 0:2],
            in_=colA[0:32, 0:2])
        k.wait("t", sm_v)
        mm(ps[:, 3072:3073], mselT[:], mcolb[0:32, 0:1], sem=None)
        mm(ps[:, 3073:3074], mselT[:], mcolb[0:32, 1:2], sem=sm_t)
        k.wait("v", sm_t)
        vop("tensor_copy", sem=sm_v, out=csr[:, 0:2], in_=ps[:, 3072:3074])
        for dst, nm in ((bQ, "L0_attn_bq"), (bK, "L0_attn_bk"),
                        (bV, "L0_attn_bv"), (bO, "L0_attn_bo")):
            k.dma("y", dst[:], hw[nm].rearrange("(c p) -> p c", p=128),
                  sm_d, allow_slow_non_contiguous=True)
        k.wait("v", sm_d)
        with contextlib.ExitStack() as sc:
            rt = SBx(sc, "rt", [128, 128], BF16)
            wnat = SBx(sc, "wnat", [128, 2 * D], BF16)
            wstg = SBx(sc, "wstg", [128, 2 * D], BF16)
            vop("tensor_scalar", out=csr[:, 2:3], in0=csr[:, 1:2],
                scalar1=alt[:, 0:1], scalar2=None, op0=ALU.mult)
            vop("tensor_scalar", out=rt[:], in0=ib[:], scalar1=csr[:, 0:1],
                scalar2=None, op0=ALU.mult)
            vop("scalar_tensor_tensor", sem=sm_v, out=rt[:], in0=swb[:],
                scalar=csr[:, 2:3], in1=rt[:], op0=ALU.mult, op1=ALU.add)
            rt_done = sm_v.v

            def prep_l0_w(src, dst, fold_rope, bv_accum):
                t_marks, v_marks, d_marks = [], [], []
                k.wait("g", sm_t)
                k.wait("v", sm_d)
                for oc in range(NCH):
                    half = oc % 2
                    if oc >= 2:
                        k.wait("g", sm_t, t_marks[oc - 2])
                    k.dma("g", wnat[:, half * D:(half + 1) * D],
                          src[oc * 128:(oc + 1) * 128, :], sm_g)
                    k.wait("t", sm_g)
                    if fold_rope:
                        k.wait("t", sm_v, rt_done)
                    if oc >= 1:
                        k.wait("t", sm_v, v_marks[oc - 1])
                    for ic in range(NCH):
                        off = (ic // 2) * 512 + (ic % 2) * 128
                        mm(ps[:, off:off + 128],
                           wnat[:, half * D + ic * 128:half * D + ic * 128 + 128],
                           rt[:] if fold_rope else ib[:],
                           sem=(sm_t if ic == NCH - 1 else None))
                    t_marks.append(sm_t.v)
                    k.wait("v", sm_t)
                    if oc >= 2:
                        k.wait("v", sm_d, d_marks[oc - 2])
                    for ic in range(NCH):
                        off = (ic // 2) * 512 + (ic % 2) * 128
                        vop("tensor_copy",
                            sem=(sm_v if ic == NCH - 1 else None),
                            out=wstg[:, half * D + ic * 128:
                                     half * D + ic * 128 + 128],
                            in_=ps[:, off:off + 128])
                    v_marks.append(sm_v.v)
                    if bv_accum:
                        k.wait("t", sm_v)
                        for ic in range(NCH):
                            mm(ps[:, 3072 + oc:3072 + oc + 1],
                               wstg[:, half * D + ic * 128:
                                    half * D + ic * 128 + 128],
                               mcolb[:, 2 * ic:2 * ic + 1],
                               start=(ic == 0), stop=(ic == NCH - 1),
                               sem=(sm_t if ic == NCH - 1 else None))
                    k.wait("y", sm_v, v_marks[-1])
                    dst_ap = dst[:, oc * 128:(oc + 1) * 128].rearrange(
                        "(ic p) c -> p ic c", p=128)
                    src_ap = wstg[:, half * D:(half + 1) * D].rearrange(
                        "p (ic c) -> p ic c", c=128)
                    k.dma("y", dst_ap, src_ap, sm_d)
                    d_marks.append(sm_d.v)

            prep_l0_w(hw["L0_attn_wq"], s_w0["wq"], True, False)
            prep_l0_w(hw["L0_attn_wk"], s_w0["wk"], True, False)
            prep_l0_w(hw["L0_attn_wv"], s_w0["wv"], False, False)
            prep_l0_w(hw["L0_attn_wo"], s_w0["wo"], False, False)
            for jdx, bb in ((0, bQ), (1, bK)):
                vop("tensor_copy", sem=sm_v,
                    out=mcolb[:, 0:16].rearrange("p (a b) -> p a b", b=2)[:, :, 0:1],
                    in_=bb[:, 0:8].rearrange("p (a b) -> p a b", b=1))
                k.wait("t", sm_v)
                for c in range(NCH):
                    mm(ps[:, 3584 + c:3584 + c + 1], rt[:],
                       mcolb[:, 2 * c:2 * c + 1],
                       sem=(sm_t if c == NCH - 1 else None))
                k.wait("v", sm_t)
                for c in range(NCH):
                    vop("tensor_copy", sem=(sm_v if c == NCH - 1 else None),
                        out=bb[:, c:c + 1], in_=ps[:, 3584 + c:3584 + c + 1])
            barrier()

        def ln_stats(MU, RR, x2s, scr):
            k.wait("t", sm_v)
            k.wait("s", sm_v)
            marks = []
            for c in range(NCH):
                if c >= 2:
                    k.wait("s", sm_t, marks[c - 2])
                sop("activation", sem=sm_s,
                    out=x2s[:, (c % 2) * T:(c % 2) * T + T],
                    in_=x[c][:], func=AF.Square)
                k.wait("t", sm_s)
                for tt in range(4):
                    mm(bank(tt), o128[:], xn[c][:, tt * 512:(tt + 1) * 512],
                       start=(c == 0), stop=(c == NCH - 1))
                for tt in range(4):
                    mm(bank(4 + tt), o128[:],
                       x2s[:, (c % 2) * T + tt * 512:(c % 2) * T + (tt + 1) * 512],
                       start=(c == 0), stop=(c == NCH - 1),
                       sem=(sm_t if tt == 3 else None))
                marks.append(sm_t.v)
            k.wait("v", sm_t)
            vop("tensor_scalar", out=MU[:], in0=bank(0, 4), scalar1=1.0 / D,
                scalar2=None, op0=ALU.mult)
            vop("scalar_tensor_tensor", out=scr[:], in0=MU[:], scalar=-1.0,
                op0=ALU.mult, op1=ALU.mult, in1=MU[:])
            vop("scalar_tensor_tensor", sem=sm_v, out=scr[:], in0=bank(4, 4),
                scalar=1.0 / D, op0=ALU.mult, op1=ALU.add, in1=scr[:])
            k.wait("s", sm_v)
            sop("activation", out=scr[:], in_=scr[:], func=AF.Ln,
                bias=cp[:, 0:1])
            sop("activation", sem=sm_s, out=RR[:], in_=scr[:], func=AF.Exp,
                scale=-0.5)
            k.wait("v", sm_s)

        def make_xbf():
            k.wait("v", sm_v)
            for c in range(NCH):
                vop("tensor_copy", sem=(sm_v if c == NCH - 1 else None),
                    out=xn[c][:], in_=x[c][:])

        def make_xn(MU, RR, scr):
            for c in range(NCH):
                vop("tensor_sub", out=scr[:], in0=x[c][:], in1=MU[:])
                vop("tensor_tensor", sem=(sm_v if c == NCH - 1 else None),
                    out=xn[c][:], in0=scr[:], in1=RR[:], op=ALU.mult)

        def proj_fm(src_hbm, cast, dst, bias_col, rope, ropebufs, wdb):
            smw = sm_g if cast else sm_d
            mmarks = []
            for oc in range(NCH):
                half = oc % 2
                band = src_hbm[:, oc * 128:(oc + 1) * 128].rearrange(
                    "(ic p) c -> p ic c", p=128)
                dst_ap = wdb[:, half * D:(half + 1) * D].rearrange(
                    "p (ic c) -> p ic c", c=128)
                if oc >= 2:
                    k.wait("g" if cast else "y", sm_t, mmarks[oc - 2])
                else:
                    k.wait("g" if cast else "y", sm_t)
                k.dma("g" if cast else "y", dst_ap, band, smw)
                k.wait("t", smw)
                for tt in range(4):
                    b = tt
                    if not (oc == 0 and tt == 0):
                        k.wait("t", sm_v)
                    for ic in range(NCH):
                        mm(bank(b),
                           wdb[:, half * D + ic * 128:half * D + (ic + 1) * 128],
                           xn[ic][:, tt * 512:(tt + 1) * 512],
                           start=(ic == 0), stop=(ic == NCH - 1),
                           sem=(sm_t if ic == NCH - 1 else None))
                    k.wait("v", sm_t)
                    if rope:
                        scrb, cc, ss2 = ropebufs
                        vop("tensor_copy", sem=sm_v,
                            out=scrb[:, tt * 512:(tt + 1) * 512], in_=bank(b))
                    else:
                        vop("tensor_scalar", sem=sm_v,
                            out=dst[oc][:, tt * 512:(tt + 1) * 512],
                            in0=bank(b),
                            scalar1=(bias_col[:, oc:oc + 1]
                                     if bias_col is not None else 0.0),
                            scalar2=None, op0=ALU.add)
                if rope:
                    scrb, cc, ss2 = ropebufs
                    k.wait("t", sm_v)
                    for tt in range(4):
                        mm(bank(4 + tt), swb[:],
                           scrb[:, tt * 512:(tt + 1) * 512],
                           sem=(sm_t if tt == 3 else None))
                    vop("tensor_tensor", out=dst[oc][:], in0=scrb[:],
                        in1=cc[:], op=ALU.mult)
                    k.wait("v", sm_t)
                    vop("tensor_tensor", out=scrb[:], in0=bank(4, 4),
                        in1=ss2[:], op=ALU.mult)
                    vop("tensor_tensor", sem=sm_v, out=dst[oc][:],
                        in0=dst[oc][:], in1=scrb[:], op=ALU.add)
                mmarks.append(sm_t.v)

        def proj_v(src_hbm, cast, vt, wdb, vbias=None):
            smw = sm_g if cast else sm_d
            for half in range(2):
                src_ap = src_hbm[:, half * 512:(half + 1) * 512].rearrange(
                    "(ic p) c -> p ic c", p=128)
                dst_ap = wdb[:, 0:NCH * 512].rearrange(
                    "p (ic c) -> p ic c", c=512)
                if half == 1:
                    k.wait("g" if cast else "y", sm_t)
                k.dma("g" if cast else "y", dst_ap, src_ap, smw)
                k.wait("t", smw)
                for tc in range(TCN):
                    b = 4 + (tc % 2)
                    if not (half == 0 and tc <= 1):
                        k.wait("t", sm_v)
                    for ic in range(NCH):
                        lastmm = ic == NCH - 1 and vbias is None
                        mm(bank(b), xn[ic][:, tc * 128:(tc + 1) * 128],
                           wdb[:, ic * 512:(ic + 1) * 512],
                           start=(ic == 0), stop=lastmm,
                           sem=(sm_t if lastmm else None))
                    if vbias is not None:
                        mm(bank(b), orow[0:1, 0:128],
                           vbias[0:1, half * 512:(half + 1) * 512],
                           start=False, stop=True, sem=sm_t)
                    k.wait("v", sm_t)
                    base = (tc * 16 + half * 8) * 65
                    dstap = vt[:, base:base + 8 * 65].rearrange(
                        "p (h j) -> p h j", j=65)[:, :, 0:64]
                    vop("tensor_copy", sem=sm_v, out=dstap, in_=bank(b))
            for tc in range(TCN):
                onesap = vt[:, tc * 16 * 65 + 64:(tc + 1) * 16 * 65:65]
                vop("memset", sem=(sm_v if tc == TCN - 1 else None),
                    ap=onesap, constant=1.0)

        def attention(qt, kt, vt, pt, ofm, causal):
            if SKIP_ATTN:
                return
            k.wait("t", sm_v)
            qspan = QB * 128
            for h in range(H):
                ch = h // 2
                off = (h % 2) * 64
                for g in range(NGRP):
                    qts = [g * QB + i for i in range(QB)]
                    for i, qtt in enumerate(qts):
                        ntile = 4 if not causal else qtt // 4 + 1
                        for j in range(ntile):
                            b = (i * 4 + j) % 2
                            k.wait("t", sm_v)
                            klo = j * 512
                            kn = 512
                            if causal and j == ntile - 1:
                                kn = 128 * (qtt + 1) - klo
                            mm(ps[:, b * 512:b * 512 + kn],
                               qt[ch][off:off + 64,
                                      qtt * 128:(qtt + 1) * 128],
                               kt[ch][off:off + 64, klo:klo + kn],
                               sem=sm_t)
                            k.wait("v", sm_t)
                            vop("tensor_reduce", sem=sm_v,
                                out=colA[:, j:j + 1],
                                in_=ps[:, b * 512:b * 512 + kn],
                                axis=AX.X, op=ALU.max)
                        vop("tensor_reduce", sem=sm_v, out=mcolb[:, i:i + 1],
                            in_=colA[:, 0:ntile], axis=AX.X, op=ALU.max)
                    vop("tensor_reduce", sem=sm_v, out=mcolb[:, QB:QB + 1],
                        in_=mcolb[:, 0:QB], axis=AX.X, op=ALU.max)
                    k.wait("t", sm_v)
                    mm(ps[0:1, 3584:3584 + 128], mcolb[:, QB:QB + 1], ib[:],
                       sem=sm_t)
                    k.wait("v", sm_t)
                    vop("tensor_reduce", out=rowm[0:1, 0:1],
                        in_=ps[0:1, 3584:3584 + 128], axis=AX.X, op=ALU.max)
                    vpad(sem=sm_v)
                    k.wait("t", sm_v)
                    mm(ps[:, 3712:3713], mor[0:1, 0:128], rowm[0:1, 0:1],
                       sem=sm_t)
                    k.wait("v", sm_t)
                    gslot = g % 2
                    vop("scalar_tensor_tensor", out=colA[:, 4:5],
                        in0=ps[:, 3712:3713], scalar=1.0, op0=ALU.mult,
                        op1=ALU.min, in1=cp[:, 12:13])
                    vpad()
                    vop("tensor_tensor", out=cbias[:, gslot:gslot + 1],
                        in0=colA[:, 4:5], in1=cp[:, 13:14], op=ALU.max)
                    vpad(sem=sm_v)
                    cb_ready = sm_v.v
                    if ATTN_PH < 2:
                        continue
                    nchunks = TCN if not causal else QB * g + QB
                    npairs = (nchunks + 1) // 2
                    for pr in range(npairs):
                        pb = 2 + 2 * (pr % 2)
                        k.wait("t", sm_s)
                        for cidx in (2 * pr, 2 * pr + 1):
                            if cidx >= nchunks:
                                continue
                            last = cidx == min(2 * pr + 1, nchunks - 1)
                            mm(ps[:, (pb + cidx % 2) * 512:
                                  (pb + cidx % 2) * 512 + qspan],
                               kt[ch][off:off + 64,
                                      cidx * 128:(cidx + 1) * 128],
                               qt[ch][off:off + 64,
                                      g * qspan:(g + 1) * qspan],
                               sem=(sm_t if last else None))
                        k.wait("s", sm_t)
                        k.wait("s", sm_v, cb_ready)
                        for cidx in (2 * pr, 2 * pr + 1):
                            if cidx >= nchunks or ATTN_NOEXP:
                                continue
                            sop("activation",
                                sem=(sm_s if cidx == min(2 * pr + 1, nchunks - 1)
                                     else None),
                                out=pt[:, cidx * qspan:(cidx + 1) * qspan],
                                in_=ps[:, (pb + cidx % 2) * 512:
                                       (pb + cidx % 2) * 512 + qspan],
                                func=AF.Exp, scale=0.125,
                                bias=cbias[:, gslot:gslot + 1])
                    if causal:
                        k.wait("v", sm_s)
                        for i, qtt in enumerate(qts):
                            vop("tensor_tensor", sem=sm_v,
                                out=pt[:, qtt * qspan + i * 128:
                                       qtt * qspan + (i + 1) * 128],
                                in0=pt[:, qtt * qspan + i * 128:
                                       qtt * qspan + (i + 1) * 128],
                                in1=tri[:], op=ALU.mult)
                        vpad(sem=sm_v)
                        k.wait("t", sm_v)
                        if h == 0 and g == 0:
                            dump("p1a", [pt[:, 0:4 * qspan], mcolb[:],
                                         cbias[:], rowm[:]])
                    else:
                        k.wait("t", sm_s)
                    if ATTN_PH < 3:
                        continue
                    for cidx in range(nchunks):
                        qlo = 0 if (not causal or cidx < QB * g) \
                            else (cidx - QB * g) * 128
                        slot = (cidx * 16 + h) if causal else (h * 16 + cidx)
                        mm(ps[0:65, 6 * 512 + qlo:6 * 512 + qspan],
                           vt[:, slot * 65:slot * 65 + 65],
                           pt[:, cidx * qspan + qlo:(cidx + 1) * qspan],
                           start=(cidx == 0), stop=(cidx == nchunks - 1),
                           sem=(sm_t if cidx == nchunks - 1 else None))
                    k.wait("v", sm_t)
                    if causal and h == 0 and g == 0:
                        vop("tensor_copy", sem=sm_v,
                            out=pt[0:65, 5 * 512:5 * 512 + qspan],
                            in_=ps[0:65, 6 * 512:6 * 512 + qspan])
                        dump("p1b", [pt[0:65, 5 * 512:5 * 512 + qspan]])
                    if ATTN_PH < 4:
                        continue
                    vop("reciprocal", sem=sm_v, out=rsrow[0:1, 0:qspan],
                        in_=ps[64:65, 6 * 512:6 * 512 + qspan])
                    k.wait("t", sm_v)
                    mm(ps[0:64, 7 * 512:7 * 512 + qspan], orow[0:1, 0:64],
                       rsrow[0:1, 0:qspan], sem=sm_t)
                    k.wait("v", sm_t)
                    vop("tensor_copy", out=rbc[:, 0:qspan],
                        in_=ps[0:64, 7 * 512:7 * 512 + qspan])
                    vop("tensor_tensor", sem=sm_v,
                        out=ofm[ch][off:off + 64, g * qspan:(g + 1) * qspan],
                        in0=ps[0:64, 6 * 512:6 * 512 + qspan],
                        in1=rbc[:, 0:qspan], op=ALU.mult)
                    k.wait("t", sm_v)

        def oproj(src_hbm, cast, src, bias_col, ybf, wdb, unperm=False):
            smw = sm_g if cast else sm_d
            mmarks = []
            for oc in range(NCH):
                half = oc % 2
                band = src_hbm[:, oc * 128:(oc + 1) * 128].rearrange(
                    "(ic p) c -> p ic c", p=128)
                dst_ap = wdb[:, half * D:(half + 1) * D].rearrange(
                    "p (ic c) -> p ic c", c=128)
                if oc >= 2:
                    k.wait("g" if cast else "y", sm_t, mmarks[oc - 2])
                else:
                    k.wait("g" if cast else "y", sm_t)
                k.dma("g" if cast else "y", dst_ap, band, smw)
                k.wait("t", smw)
                for tt in range(4):
                    b = tt
                    if not (oc == 0 and tt == 0):
                        k.wait("t", sm_v)
                    for ic in range(NCH):
                        mm(bank(b),
                           wdb[:, half * D + ic * 128:half * D + (ic + 1) * 128],
                           src[ic][:, tt * 512:(tt + 1) * 512],
                           start=(ic == 0), stop=(ic == NCH - 1),
                           sem=(sm_t if ic == NCH - 1 else None))
                    k.wait("v", sm_t)
                    if unperm:
                        dst_ap = ybf[oc].rearrange(
                            "p (a c) -> p c a", c=16)[:, 4 * tt:4 * tt + 4, :]
                        src_ap = bank(b).rearrange("p (c a) -> p c a", a=128)
                    else:
                        dst_ap = ybf[oc][:, tt * 512:(tt + 1) * 512]
                        src_ap = bank(b)
                    vop("tensor_scalar", sem=sm_v,
                        out=dst_ap, in0=src_ap,
                        scalar1=(bias_col[:, oc:oc + 1]
                                 if bias_col is not None else 0.0),
                        scalar2=None, op0=ALU.add)
                mmarks.append(sm_t.v)

        def post_ln_add(ybf, MU, RR, x2s, scr, t0, tn):
            tw = tn - t0
            ntt = tw // 512
            k.wait("t", sm_v)
            k.wait("s", sm_v)
            marks = []
            for c in range(NCH):
                if c >= 2:
                    k.wait("s", sm_t, marks[c - 2])
                sop("activation", sem=sm_s,
                    out=x2s[:, (c % 2) * tw:(c % 2) * tw + tw],
                    in_=ybf[c][:, 0:tw], func=AF.Square)
                k.wait("t", sm_s)
                for tt in range(ntt):
                    mm(bank(tt), o128[:], ybf[c][:, tt * 512:(tt + 1) * 512],
                       start=(c == 0), stop=(c == NCH - 1))
                for tt in range(ntt):
                    mm(bank(4 + tt), o128[:],
                       x2s[:, (c % 2) * tw + tt * 512:
                           (c % 2) * tw + (tt + 1) * 512],
                       start=(c == 0), stop=(c == NCH - 1),
                       sem=(sm_t if tt == ntt - 1 else None))
                marks.append(sm_t.v)
            k.wait("v", sm_t)
            vop("tensor_scalar", out=MU[:, 0:tw], in0=ps[:, 0:tw],
                scalar1=1.0 / D, scalar2=None, op0=ALU.mult)
            vop("scalar_tensor_tensor", out=scr[:, 0:tw], in0=MU[:, 0:tw],
                scalar=-1.0, op0=ALU.mult, op1=ALU.mult, in1=MU[:, 0:tw])
            vop("scalar_tensor_tensor", sem=sm_v, out=scr[:, 0:tw],
                in0=ps[:, 4 * 512:4 * 512 + tw], scalar=1.0 / D,
                op0=ALU.mult, op1=ALU.add, in1=scr[:, 0:tw])
            k.wait("s", sm_v)
            sop("activation", out=scr[:, 0:tw], in_=scr[:, 0:tw], func=AF.Ln,
                bias=cp[:, 0:1])
            sop("activation", sem=sm_s, out=RR[:, 0:tw], in_=scr[:, 0:tw],
                func=AF.Exp, scale=-0.5)
            k.wait("v", sm_s)
            for c in range(NCH):
                vop("tensor_sub", out=scr[:, 0:tw], in0=ybf[c][:, 0:tw],
                    in1=MU[:, 0:tw])
                vop("tensor_tensor", out=scr[:, 0:tw], in0=scr[:, 0:tw],
                    in1=RR[:, 0:tw], op=ALU.mult)
                vop("tensor_add", sem=(sm_v if c == NCH - 1 else None),
                    out=x[c][:, t0:tn], in0=x[c][:, t0:tn],
                    in1=scr[:, 0:tw])

        def x_store_fm():
            k.wait("y", sm_v)
            for c in range(NCH):
                k.dma("y", s_x[c * 128:(c + 1) * 128, :], x[c][:], sm_d)
            barrier()

        def x_load_fm():
            for c in range(NCH):
                k.dma("y", x[c][:], s_x[c * 128:(c + 1) * 128, :], sm_d)
            barrier()

        def attn_ln(layer):
            L = layer
            with contextlib.ExitStack() as sc:
                x2s = SBx(sc, f"a_x2s_{L}", [128, 2 * T], BF16)
                MU = SBx(sc, f"a_MU_{L}", [128, T], F32)
                RR = SBx(sc, f"a_RR_{L}", [128, T], BF16)
                scr = SBx(sc, f"a_scr_{L}", [128, T], F32)
                make_xbf()
                ln_stats(MU, RR, x2s, scr)
                dump(f"st{layer}", [MU[:], RR[:]]) if f"st{layer}" in DBG else None
                make_xn(MU, RR, scr)
                barrier()
                dump(f"xn{layer}", [b[:] for b in xn])

        def attn_core(layer):
            L = layer
            with contextlib.ExitStack() as sc:
                if layer == 0:
                    qtP = [SBx(sc, f"a_qp{c}_{L}", [128, T], BF16)
                           for c in range(NCH)]
                    ktP = [SBx(sc, f"a_kp{c}_{L}", [128, T], BF16)
                           for c in range(NCH)]
                sci = contextlib.ExitStack()
                qt = [SBx(sci, f"a_qt{c}_{L}", [128, T], BF16)
                      for c in range(NCH)]
                kt = [SBx(sci, f"a_kt{c}_{L}", [128, T], BF16)
                      for c in range(NCH)]
                with contextlib.ExitStack() as sc2:
                    wdb = SBx(sc2, f"a_wdb_{L}", [128, 2 * D], BF16)
                    if layer == 0:
                        proj_fm(s_w0["wq"], False, qt, bQ, False, None, wdb)
                        proj_fm(s_w0["wk"], False, kt, bK, False, None, wdb)
                    else:
                        cc = SBx(sc2, f"a_cc_{L}", [128, T], BF16)
                        ss2 = SBx(sc2, f"a_ss2_{L}", [128, T], BF16)
                        scrb = SBx(sc2, f"a_scrb_{L}", [128, T], BF16)
                        k.dma("y", cc[:], cc_c.ap(), sm_d)
                        k.dma("y", ss2[:], ss2_c.ap(), sm_d)
                        k.wait("v", sm_d)
                        proj_fm(hw["L1_attn_Wq"], True, qt, None, True,
                                (scrb, cc, ss2), wdb)
                        proj_fm(hw["L1_attn_Wk"], True, kt, None, True,
                                (scrb, cc, ss2), wdb)
                    barrier()
                dump(f"qt{layer}", [b[:] for b in qt])
                dump(f"kt{layer}", [b[:] for b in kt])
                if layer == 0:
                    k.wait("t", sm_v)
                    bmarks = []
                    for mat_src, mat_dst in ((qt, qtP), (kt, ktP)):
                        for ch2 in range(NCH):
                            for ck in range(16):
                                b2 = (ch2 * 16 + ck) % 2
                                if len(bmarks) >= 2:
                                    k.wait("t", sm_v, bmarks[-2])
                                bs = (ck % 2) * 64
                                e_lo = ib if bs == 0 else hswp
                                e_hi = hswp if bs == 0 else ib
                                blkc = mat_src[ck // 2]
                                mm(ps[:, b2 * 512:b2 * 512 + 128],
                                   e_lo[bs:bs + 64, :],
                                   blkc[bs:bs + 64,
                                        (2 * ch2) * 128:(2 * ch2 + 1) * 128],
                                   start=True, stop=False)
                                mm(ps[:, b2 * 512:b2 * 512 + 128],
                                   e_hi[bs:bs + 64, :],
                                   blkc[bs:bs + 64, (2 * ch2 + 1) * 128:
                                        (2 * ch2 + 2) * 128],
                                   start=False, stop=True, sem=sm_t)
                                k.wait("v", sm_t)
                                vop("tensor_copy", sem=sm_v,
                                    out=mat_dst[ch2][:,
                                                     ck * 128:(ck + 1) * 128],
                                    in_=ps[:, b2 * 512:b2 * 512 + 128])
                                bmarks.append(sm_v.v)
                    barrier()
                    sci.close()
                else:
                    sc.enter_context(sci)
                vt = SBx(sc, f"a_vt_{L}", [128, TCN * H * 65], BF16)
                with contextlib.ExitStack() as sc2:
                    if layer == 0:
                        bvrb = SBx(sc2, f"a_bvrb_{L}", [1, D], BF16)
                        with contextlib.ExitStack() as sc3:
                            bvr = SBx(sc3, f"a_bvr_{L}", [1, D], F32)
                            k.dma("y", bvr[:],
                                  hw["L0_attn_bv"].rearrange(
                                      "(a b) -> a b", a=1), sm_d)
                            k.wait("v", sm_d)
                            vop("tensor_copy", sem=sm_v, out=bvrb[:],
                                in_=bvr[:])
                            k.wait("y", sm_v)
                            barrier()
                        wdb = SBx(sc2, f"a_wdb2_{L}", [128, NCH * 512],
                                  BF16)
                        proj_v(s_w0["wv"], False, vt, wdb, vbias=bvrb)
                    else:
                        wdb = SBx(sc2, f"a_wdb2_{L}", [128, NCH * 512],
                                  BF16)
                        proj_v(hw["L1_attn_Wv"], True, vt, wdb)
                    barrier()
                with contextlib.ExitStack() as sc2:
                    pt = SBx(sc2, f"a_pt_{L}", [128, TCN * QB * 128], BF16)
                    ofm = xn
                    dump(f"vt{layer}", [vt[:]])
                    if layer == 0:
                        attention(qtP, ktP, vt, pt, ofm, causal=False)
                    else:
                        attention(qt, kt, vt, pt, ofm, causal=True)
                    barrier()
                    dump(f"ofm{layer}", [b[:] for b in ofm])

        def attn_out(layer):
            L = layer
            with contextlib.ExitStack() as sc:
                ybf = [SBx(sc, f"a_yb{c}_{L}", [128, T], BF16) for c in range(NCH)]
                x2s = SBx(sc, f"a_x2s2_{L}", [128, 2 * T], BF16)
                MU = SBx(sc, f"a_MU2_{L}", [128, T], F32)
                RR = SBx(sc, f"a_RR2_{L}", [128, T], BF16)
                scr = SBx(sc, f"a_scr2_{L}", [128, T], F32)
                wdb = SBx(sc, f"a_wdb3_{L}", [128, 2 * D], BF16)
                if layer == 0:
                    oproj(s_w0["wo"], False, xn, bO, ybf, wdb, unperm=True)
                else:
                    oproj(hw["L1_attn_Wo"], True, xn, None, ybf, wdb)
                dump(f"oy{layer}", [b[:] for b in ybf])
                post_ln_add(ybf, MU, RR, x2s, scr, 0, T)
                barrier()

        def quant_ints(dst, MU, scr, M32, dtmp, gqi, rowC, rowbfA):
            k.wait("v", sm_v)
            for c in range(NCH):
                vop("tensor_sub", out=dtmp[:], in0=x[c][:], in1=MU[:])
                if c == 0:
                    vop("scalar_tensor_tensor", out=M32, in0=dtmp[:],
                        scalar=-1.0, op0=ALU.mult, op1=ALU.max, in1=dtmp[:])
                else:
                    vop("scalar_tensor_tensor", out=scr[:], in0=dtmp[:],
                        scalar=-1.0, op0=ALU.mult, op1=ALU.max, in1=dtmp[:])
                    vop("tensor_max", out=M32, in0=M32, in1=scr[:])
            for n in (64, 32):
                vop("tensor_copy", out=scr[0:n, :], in_=M32[n:2 * n, :])
                vop("tensor_max", out=M32[0:n, :], in0=M32[0:n, :],
                    in1=scr[0:n, :])
            for n in (16, 8, 4, 2, 1):
                mask = [(i + n) % 32 for i in range(32)]
                vop("stream_shuffle", out=scr[0:32, :], in_=M32[0:32, :],
                    mask=mask)
                vop("tensor_max", out=M32[0:32, :], in0=M32[0:32, :],
                    in1=scr[0:32, :])
            vop("tensor_scalar", out=rowC[:], in0=M32[0:1, :], scalar1=1e-5,
                scalar2=None, op0=ALU.max)
            vop("reciprocal", out=scr[0:1, :], in_=rowC[:])
            vop("tensor_scalar", sem=sm_v, out=scr[0:1, :], in0=scr[0:1, :],
                scalar1=127.0, scalar2=None, op0=ALU.mult)
            k.wait("t", sm_v)
            for tt in range(4):
                mm(bank(tt), orf[:],
                   scr[0:1, tt * 512:(tt + 1) * 512],
                   sem=(sm_t if tt == 3 else None))
            k.wait("v", sm_t)
            k.wait("s", sm_t)
            for c in range(NCH):
                vop("tensor_sub", out=dtmp[:], in0=x[c][:], in1=MU[:])
                vop("tensor_tensor", sem=sm_v, out=dtmp[:], in0=dtmp[:],
                    in1=bank(0, 4), op=ALU.mult)
                k.wait("s", sm_v)
                sop("activation", sem=sm_s, out=gqi[:], in_=dtmp[:],
                    func=AF.Copy, bias=-1.0)
                k.wait("v", sm_s)
                vop("tensor_sub", out=scr[:], in0=dtmp[:], in1=gqi[:])
                vop("scalar_tensor_tensor", out=dtmp[:], in0=scr[:],
                    scalar=1.5, op0=ALU.is_gt, op1=ALU.add, in1=gqi[:])
                vop("scalar_tensor_tensor", sem=sm_v, out=dst[c][:],
                    in0=scr[:], scalar=0.5, op0=ALU.is_ge, op1=ALU.add,
                    in1=dtmp[:])
                k.wait("s", sm_v)

        def tern_gen(src, rows, cols, dst, swidx, mwslot, wfb, wfs, trs, trs2):
            nrc = rows // 128
            marks = []
            for rc in range(nrc):
                half = rc % 2
                if rc >= 2:
                    k.wait("g", sm_t, marks[rc - 2])
                k.dma("g", wfb[:, half * 4096:half * 4096 + cols],
                      src[rc * 128:(rc + 1) * 128, :], sm_g)
                k.wait("v", sm_g)
                k.wait("v", sm_t)
                vop("tensor_relu", sem=sm_v, out=trs[:, 0:cols],
                    in_=wfb[:, half * 4096:half * 4096 + cols])
                k.wait("t", sm_v)
                last_rc = rc == nrc - 1
                for tt in range(cols // 512):
                    last = last_rc and tt == cols // 512 - 1
                    mm(bank(0), o128[:], trs[:, tt * 512:(tt + 1) * 512],
                       start=(rc == 0 and tt == 0), stop=last)
                    mm(bank(1), o128[:],
                       wfb[:, half * 4096 + tt * 512:
                           half * 4096 + (tt + 1) * 512],
                       start=(rc == 0 and tt == 0), stop=last,
                       sem=(sm_t if tt == cols // 512 - 1 else None))
                marks.append(sm_t.v)
            k.wait("v", sm_t)
            vop("tensor_reduce", out=colC[:, 0:1], in_=bank(0),
                axis=AX.X, op=ALU.add)
            vop("tensor_reduce", out=colC[:, 1:2], in_=bank(1),
                axis=AX.X, op=ALU.add)
            vop("scalar_tensor_tensor", out=colC[:, 2:3],
                in0=colC[:, 0:1], scalar=2.0, op0=ALU.mult,
                op1=ALU.subtract, in1=colC[:, 1:2])
            vpad()
            vop("scalar_tensor_tensor", out=colC[:, 5:6],
                in0=colC[:, 2:3], scalar=1.0 / (rows * cols),
                op0=ALU.mult, op1=ALU.max, in1=cp[:, 3:4])
            vpad()
            if mwslot is not None:
                vop("scalar_tensor_tensor", out=swcol[:, 3:4],
                    in0=colC[:, 5:6], scalar=1.0 / 254.0, op0=ALU.mult,
                    op1=ALU.add, in1=cp[:, 9:10])
            vop("reciprocal", sem=sm_v, out=swcol[:, swidx:swidx + 1],
                in_=colC[:, 5:6])
            k.wait("s", sm_v)
            k.wait("y", sm_v)
            dump(f"ftc{swidx}", [colC[:, 0:8], swcol[:, 0:4]])
            dmarks = []
            smarks = []
            for rc in range(nrc):
                if rc >= 1:
                    k.wait("y", sm_s, smarks[-1])
                k.dma("y", wfs[:, 0:cols], src[rc * 128:(rc + 1) * 128, :],
                      sm_d)
                k.wait("s", sm_d)
                sop("activation", out=trs[:, 0:cols], in_=wfs[:, 0:cols],
                    func=AF.Sign, scale=swcol[:, swidx:swidx + 1],
                    bias=cp[:, 5:6])
                sop("activation", sem=sm_s, out=trs2[:, 0:cols],
                    in_=wfs[:, 0:cols], func=AF.Sign,
                    scale=swcol[:, swidx:swidx + 1], bias=cp[:, 6:7])
                smarks.append(sm_s.v)
                k.wait("v", sm_s)
                vop("tensor_add", sem=sm_v, out=trs[:, 0:cols],
                    in0=trs[:, 0:cols], in1=trs2[:, 0:cols])
                for jb in range(cols // 512):
                    k.wait("t", sm_v)
                    if dmarks:
                        k.wait("t", sm_d, dmarks[-1])
                    for j4 in range(4):
                        jj = jb * 4 + j4
                        mm(ps[:, 2048 + j4 * 128:2048 + (j4 + 1) * 128],
                           trs[:, jj * 128:(jj + 1) * 128], ib[:],
                           sem=(sm_t if j4 == 3 else None))
                    k.wait("v", sm_t)
                    vop("tensor_copy", sem=sm_v, out=trs2[:, 0:512],
                        in_=ps[:, 2048:2560])
                    k.wait("y", sm_v)
                    dst_ap = dst[jb * 512:(jb + 1) * 512,
                                 rc * 128:(rc + 1) * 128].rearrange(
                        "(j p) c -> p j c", p=128)
                    src_ap = trs2[:, 0:512].rearrange("p (j c) -> p j c",
                                                      c=128)
                    k.dma("y", dst_ap, src_ap, sm_d)
                    dmarks.append(sm_d.v)
                k.wait("s", sm_d, dmarks[-1])

        def ffn0_block():
            with contextlib.ExitStack() as sc:
                xdq = [SBx(sc, f"f_xdq{c}", [128, T], BF16)
                       for c in range(NCH)]
                scR = contextlib.ExitStack()
                rowB = SBx(scR, "f_rowB", [1, T], F32)
                rowC = SBx(scR, "f_rowC", [1, T], F32)
                rowbfA = SBx(scR, "f_rbfA", [1, T], BF16)
                rowbfB = SBx(scR, "f_rbfB", [1, T], BF16)
                with contextlib.ExitStack() as sc2:
                    x2s = SBx(sc2, "f_x2s", [128, 2 * T], BF16)
                    MU = SBx(sc2, "f_MU", [128, T], F32)
                    RR = SBx(sc2, "f_RR", [128, T], BF16)
                    scr = SBx(sc2, "f_scr", [128, T], F32)
                    dtmp = SBx(sc2, "f_dtmp", [128, T], F32)
                    gqi = SBx(sc2, "f_gqi", [128, T], I32)
                    make_xbf()
                    ln_stats(MU, RR, x2s, scr)
                    vop("tensor_copy", sem=sm_v, out=rowB[:], in_=RR[0:1, :])
                    M32 = x2s[:, 0:2 * T].bitcast(F32)
                    quant_ints(xdq, MU, scr, M32, dtmp, gqi, rowC, rowbfA)
                    vop("tensor_tensor", sem=sm_v, out=rowC[:],
                        in0=rowB[:], in1=rowC[:], op=ALU.mult)
                    barrier()
                dump("fxdq", [b[:] for b in xdq])
                with contextlib.ExitStack() as sc2:
                    wfb = SBx(sc2, "f_wfb", [128, 2 * 4096], BF16)
                    wfs = SBx(sc2, "f_wfs", [128, 4096], F32)
                    trs = SBx(sc2, "f_trs", [128, 4096], BF16)
                    trs2 = SBx(sc2, "f_trs2", [128, 4096], BF16)
                    tern_gen(hw["L0_ffn_w_gate"], INNER, D, s_tg, 0, 0,
                             wfb, wfs, trs, trs2)
                    tern_gen(hw["L0_ffn_w_up"], INNER, D, s_tu, 1, None,
                             wfb, wfs, trs, trs2)
                    tern_gen(hw["L0_ffn_w_down"], D, INNER, s_td, 2, None,
                             wfb, wfs, trs, trs2)
                    barrier()
                dump_dram("ftg", s_tg, [D, INNER], BF16)
                dump("ftrows", [rowm[:], colA[:], swcol[:], cp[:]])

                k.wait("y", sm_v)
                k.dma("y", s_row.rearrange("(o t) -> o t", o=1), rowC[:], sm_d)
                k.wait("y", sm_d)
                k.dma("y", acol[:, 0:TCN],
                      s_row.rearrange("(tc p) -> p tc", p=128), sm_d,
                      allow_slow_non_contiguous=True)
                barrier()
                vop("tensor_scalar", sem=sm_v, out=acol[:, 0:TCN],
                    in0=acol[:, 0:TCN], scalar1=swcol[:, 3:4], scalar2=None,
                    op0=ALU.mult)
                k.wait("s", sm_v)
                k.wait("t", sm_v)
                k.wait("y", sm_v)
                dump("facol", [acol[:, 0:TCN]])
                scR.close()
                with contextlib.ExitStack() as sc2:
                    gub = SBx(sc2, "f_gu", [128, 2 * INNER], F32)
                    gdb = SBx(sc2, "f_gd", [128, 1024], F32)
                    gqib = SBx(sc2, "f_gqib", [128, 1024], I32)
                    tstr = SBx(sc2, "f_ts1", [128, 4096], BF16)
                    tstrd = SBx(sc2, "f_ts2", [128, 4096], BF16)
                    gqb = tstrd
                    ybfs = [SBx(sc2, f"f_yb{c}", [128, 512], BF16)
                            for c in range(NCH)]
                    y2s = SBx(sc2, "f_y2s", [128, 2 * 512], BF16)
                    MUs = SBx(sc2, "f_MUs", [128, 512], F32)
                    RRs = SBx(sc2, "f_RRs", [128, 512], BF16)
                    scs = SBx(sc2, "f_scs", [128, 512], F32)
                    gqt = [xn[i // 4][:, (i % 4) * 512:(i % 4 + 1) * 512]
                           for i in range(32)]
                    for sl in range(4):
                        for tcp in range(2):
                            tcs = [sl * 4 + tcp * 2, sl * 4 + tcp * 2 + 1]
                            for it in range(8):
                                gsrc = s_tg[:, it * 512:(it + 1) * 512] \
                                    .rearrange("(ic p) j -> p ic j", p=128)
                                usrc = s_tu[:, it * 512:(it + 1) * 512] \
                                    .rearrange("(ic p) j -> p ic j", p=128)
                                k.wait("y", sm_t)
                                k.dma("y", tstr[:].rearrange(
                                    "p (ic j) -> p ic j", j=512), gsrc, sm_d)
                                k.dma("y", tstrd[:].rearrange(
                                    "p (ic j) -> p ic j", j=512), usrc, sm_d)
                                k.wait("t", sm_d)
                                for ti, tc in enumerate(tcs):
                                    bg = 2 * ti
                                    bu = 2 * ti + 1
                                    k.wait("t", sm_v)
                                    for ic in range(NCH):
                                        mm(bank(bg),
                                           xdq[ic][:, tc * 128:(tc + 1) * 128],
                                           tstr[:, ic * 512:(ic + 1) * 512],
                                           start=(ic == 0),
                                           stop=(ic == NCH - 1),
                                           sem=(sm_t if ic == NCH - 1
                                                else None))
                                    for ic in range(NCH):
                                        mm(bank(bu),
                                           xdq[ic][:, tc * 128:(tc + 1) * 128],
                                           tstrd[:, ic * 512:(ic + 1) * 512],
                                           start=(ic == 0),
                                           stop=(ic == NCH - 1),
                                           sem=(sm_t if ic == NCH - 1
                                                else None))
                                    k.wait("s", sm_t)
                                    sop("activation", sem=sm_s,
                                        out=gub[:, ti * INNER + it * 512:
                                                ti * INNER + (it + 1) * 512],
                                        in_=bank(bg), func=AF.Silu,
                                        scale=acol[:, tc:tc + 1])
                                    k.wait("v", sm_s)
                                    vop("tensor_tensor", sem=sm_v,
                                        out=gub[:, ti * INNER + it * 512:
                                                ti * INNER + (it + 1) * 512],
                                        in0=gub[:, ti * INNER + it * 512:
                                                ti * INNER + (it + 1) * 512],
                                        in1=bank(bu), op=ALU.mult)
                            for ti2 in range(2):
                                vop("tensor_reduce",
                                    out=colA[:, ti2:ti2 + 1],
                                    in_=gub[:, ti2 * INNER:
                                            (ti2 + 1) * INNER], axis=AX.X,
                                    op=ALU.max, apply_absolute_value=True)
                            for ti2 in range(2):
                                vop("scalar_tensor_tensor",
                                    out=colA[:, 2 + ti2:3 + ti2],
                                    in0=colA[:, ti2:ti2 + 1],
                                    scalar=1.0, op0=ALU.mult, op1=ALU.max,
                                    in1=cp[:, 3:4])
                            vpad()
                            for ti2 in range(2):
                                vop("reciprocal",
                                    out=colA[:, 4 + ti2:5 + ti2],
                                    in_=colA[:, 2 + ti2:3 + ti2])
                            vpad()
                            for ti2 in range(2):
                                vop("scalar_tensor_tensor",
                                    out=colA[:, 6 + ti2:7 + ti2],
                                    in0=colA[:, 4 + ti2:5 + ti2],
                                    scalar=127.0, op0=ALU.mult, op1=ALU.add,
                                    in1=cp[:, 9:10])
                            vpad(sem=sm_v)
                            for ti, tc in enumerate(tcs):
                                g0 = ti * INNER
                                sc_ap = colA[:, 6 + ti:7 + ti]
                                k.wait("s", sm_v)
                                k.wait("v", sm_t)
                                for hh in range(4):
                                    hsl = slice(g0 + hh * 1024,
                                                g0 + (hh + 1) * 1024)
                                    sop("activation", sem=sm_s, out=gqib[:],
                                        in_=gub[:, hsl], func=AF.Copy,
                                        scale=sc_ap, bias=-1.0)
                                    k.wait("v", sm_s)
                                    vop("scalar_tensor_tensor", out=gdb[:],
                                        in0=gub[:, hsl], scalar=sc_ap,
                                        op0=ALU.mult, op1=ALU.subtract,
                                        in1=gqib[:])
                                    vop("scalar_tensor_tensor",
                                        out=gub[:, hsl], in0=gdb[:],
                                        scalar=1.5, op0=ALU.is_gt,
                                        op1=ALU.add, in1=gqib[:])
                                    vop("scalar_tensor_tensor", sem=sm_v,
                                        out=gqb[:, hh * 1024:(hh + 1) * 1024],
                                        in0=gdb[:], scalar=0.5, op0=ALU.is_ge,
                                        op1=ALU.add, in1=gub[:, hsl])
                                    k.wait("s", sm_v)
                                ltc = tc - sl * 4
                                k.wait("t", sm_v)
                                for jb in range(8):
                                    for j4 in range(4):
                                        jj = jb * 4 + j4
                                        mm(ps[:, 2048 + j4 * 128:
                                              2048 + (j4 + 1) * 128],
                                           gqb[:, jj * 128:(jj + 1) * 128],
                                           ib[:],
                                           sem=(sm_t if j4 == 3 else None))
                                    k.wait("v", sm_t)
                                    outap = xn[jb][:, 0:T].rearrange(
                                        "p (a b) -> p a b", b=512)[
                                        :, :, ltc * 128:(ltc + 1) * 128]
                                    vop("tensor_copy", sem=sm_v, out=outap,
                                        in_=ps[:, 2048:2560].rearrange(
                                            "p (a b) -> p a b", b=128))
                                    k.wait("t", sm_v)
                        for fc in range(NCH):
                            dsrc = s_td[:, fc * 128:(fc + 1) * 128].rearrange(
                                "(ic p) j -> p ic j", p=128)
                            k.wait("y", sm_t)
                            k.dma("y", tstr[:].rearrange(
                                "p (ic j) -> p ic j", j=128), dsrc, sm_d)
                            k.wait("t", sm_d)
                            b = fc % 2
                            k.wait("t", sm_v)
                            for icn in range(32):
                                mm(bank(b),
                                   tstr[:, icn * 128:(icn + 1) * 128],
                                   gqt[icn], start=(icn == 0),
                                   stop=(icn == 31),
                                   sem=(sm_t if icn == 31 else None))
                            k.wait("v", sm_t)
                            vop("tensor_copy", sem=sm_v, out=ybfs[fc][:],
                                in_=bank(b))
                        dump(f"fy{sl}", [b[:] for b in ybfs])
                        post_ln_add(ybfs, MUs, RRs, y2s, scs, sl * 512,
                                    (sl + 1) * 512)
                    barrier()
                dump("fgqt", [b[:] for b in xn])

        def ffn1_block():
            with contextlib.ExitStack() as sc:
                rowB = SBx(sc, "g_rowB", [1, T], F32)
                rowC = SBx(sc, "g_rowC", [1, T], F32)
                rowbfA = SBx(sc, "g_rbfA", [1, T], BF16)
                xdq = [SBx(sc, f"g_xdq{c}", [128, T], BF16)
                       for c in range(NCH)]
                with contextlib.ExitStack() as sc2:
                    x2s = SBx(sc2, "g_x2s", [128, 2 * T], BF16)
                    MU = SBx(sc2, "g_MU", [128, T], F32)
                    RR = SBx(sc2, "g_RR", [128, T], BF16)
                    scr = SBx(sc2, "g_scr", [128, T], F32)
                    dtmp = SBx(sc2, "g_dtmp", [128, T], F32)
                    gqi = SBx(sc2, "g_gqi", [128, T], I32)
                    make_xbf()
                    ln_stats(MU, RR, x2s, scr)
                    vop("tensor_copy", sem=sm_v, out=rowB[:], in_=RR[0:1, :])
                    M32 = x2s[:, 0:2 * T].bitcast(F32)
                    quant_ints(xdq, MU, scr, M32, dtmp, gqi, rowC, rowbfA)
                    barrier()
                with contextlib.ExitStack() as sc2:
                    w1t = [SBx(sc2, f"g_w1t{i}", [128, D], BF16)
                           for i in range(NCH)]
                    A32 = SBx(sc2, "g_A32", [128, T], BF16)
                    b1c = SBx(sc2, "g_b1c", [128, 8], F32)
                    wsr = SBx(sc2, "g_wsr", [1, 1], F32)
                    ybf = xn
                    pk_flat = hw["L1_ffn_packed_weight"]
                    with contextlib.ExitStack() as sc3:
                        pk0 = SBx(sc3, "g_pk0", [128, 1024], I32)
                        pk1 = SBx(sc3, "g_pk1", [128, 1024], I32)
                        pki = SBx(sc3, "g_pki", [128, 1024], I32)
                        w1n = SBx(sc3, "g_w1n", [128, 1024], BF16)
                        k.dma("y", pk0[:], pk_flat[0:131072].rearrange(
                            "(p f) -> p f", f=1024), sm_d)
                        k.dma("y", pk1[:], pk_flat[131072:262144].rearrange(
                            "(p f) -> p f", f=1024), sm_d)
                        k.dma("y", wsr[:], hw["L1_ffn_weight_scale"].rearrange(
                            "(a b) -> a b", b=1), sm_d)
                        k.dma("y", b1c[:], hw["L1_ffn_bias"].rearrange(
                            "(c p) -> p c", p=128), sm_d,
                            allow_slow_non_contiguous=True)
                        k.wait("v", sm_d)
                        k.wait("t", sm_d)
                        mm(ps[:, 0:1], orf[0:1, 0:128], wsr[0:1, 0:1])
                        for oc in range(NCH):
                            src_pk = pk0 if oc % 2 == 0 else pk1
                            shift = 2 * (oc // 2)
                            vop("tensor_scalar", out=pki[:], in0=src_pk[:],
                                scalar1=shift, scalar2=3,
                                op0=ALU.logical_shift_right,
                                op1=ALU.bitwise_and)
                            vop("tensor_scalar", sem=sm_v, out=w1n[:],
                                in0=pki[:],
                                scalar1=1, scalar2=None, op0=ALU.subtract)
                            k.wait("t", sm_v)
                            for jb in range(2):
                                for j4 in range(4):
                                    jj = jb * 4 + j4
                                    mm(ps[:, 2048 + j4 * 128:
                                          2048 + (j4 + 1) * 128],
                                       w1n[:, jj * 128:(jj + 1) * 128], ib[:],
                                       sem=(sm_t if j4 == 3 else None))
                                k.wait("v", sm_t)
                                for j4 in range(4):
                                    ic = jb * 4 + j4
                                    vop("tensor_copy",
                                        sem=(sm_v if j4 == 3 else None),
                                        out=w1t[ic][:, oc * 128:(oc + 1) * 128],
                                        in_=ps[:, 2048 + j4 * 128:
                                               2048 + (j4 + 1) * 128])
                                k.wait("t", sm_v)
                        barrier()
                    vop("tensor_tensor", out=rowC[:], in0=rowB[:],
                        in1=rowC[:], op=ALU.mult)
                    vop("tensor_scalar", out=rowC[:], in0=rowC[:],
                        scalar1=1.0 / 127.0, scalar2=None, op0=ALU.mult)
                    vop("tensor_copy", sem=sm_v, out=rowbfA[:], in_=rowC[:])
                    k.wait("t", sm_v)
                    for tt in range(4):
                        mm(bank(4 + tt), orow[0:1, 0:128],
                           rowbfA[0:1, tt * 512:(tt + 1) * 512],
                           sem=(sm_t if tt == 3 else None))
                    k.wait("v", sm_t)
                    vop("tensor_copy", out=colC[:, 7:8], in_=ps[:, 0:1])
                    vop("tensor_copy", out=A32[:], in_=bank(4, 4))
                    vop("tensor_scalar", sem=sm_v, out=A32[:], in0=A32[:],
                        scalar1=colC[:, 7:8], scalar2=None, op0=ALU.mult)
                    k.wait("t", sm_v)
                    for oc in range(NCH):
                        for tt in range(4):
                            b = tt
                            if not (oc == 0 and tt == 0):
                                k.wait("t", sm_v)
                            for ic in range(NCH):
                                mm(bank(b),
                                   w1t[ic][:, oc * 128:(oc + 1) * 128],
                                   xdq[ic][:, tt * 512:(tt + 1) * 512],
                                   start=(ic == 0), stop=(ic == NCH - 1),
                                   sem=(sm_t if ic == NCH - 1 else None))
                            k.wait("v", sm_t)
                            vop("tensor_tensor",
                                out=ybf[oc][:, tt * 512:(tt + 1) * 512],
                                in0=bank(b),
                                in1=A32[:, tt * 512:(tt + 1) * 512],
                                op=ALU.mult)
                            vop("tensor_scalar", sem=sm_v,
                                out=ybf[oc][:, tt * 512:(tt + 1) * 512],
                                in0=ybf[oc][:, tt * 512:(tt + 1) * 512],
                                scalar1=b1c[:, oc:oc + 1], scalar2=None,
                                op0=ALU.add)
                    with contextlib.ExitStack() as sc3:
                        x2s = SBx(sc3, "g_x2s2", [128, 2 * T], BF16)
                        MU = SBx(sc3, "g_MU2", [128, T], F32)
                        RR = SBx(sc3, "g_RR2", [128, T], BF16)
                        scr = SBx(sc3, "g_scr3", [128, T], F32)
                        post_ln_add(ybf, MU, RR, x2s, scr, 0, T)
                    barrier()

        if not FFN_ONLY:
            attn_ln(0)
            x_store_fm()
            scx.close()
            attn_core(0)
            scx = contextlib.ExitStack()
            x_alloc(scx)
            x_load_fm()
            attn_out(0)
        if STAGE >= 2:
            ffn0_block()
        if STAGE >= 3:
            attn_ln(1)
            x_store_fm()
            scx.close()
            attn_core(1)
            scx = contextlib.ExitStack()
            x_alloc(scx)
            x_load_fm()
            attn_out(1)
        if STAGE >= 4:
            ffn1_block()

        with contextlib.ExitStack() as sc:
            stg = [SBx(sc, "ost0", [128, T], F32),
                   SBx(sc, "ost1", [128, T], F32)]
            k.wait("t", sm_v)
            store_marks = []
            emarks = []
            for pair in range(TCN // 2):
                sb_ = stg[pair % 2]
                if pair >= 2:
                    k.wait("v", sm_st, store_marks[pair - 2])
                if pair >= 1:
                    k.wait("t", sm_v, emarks[pair - 1])
                for half in range(2):
                    for c in range(NCH):
                        off = (half * 4 + (c // 2)) * 512 + (c % 2) * 128
                        mm(ps[:, off:off + 128],
                           x[c][:, (2 * pair + half) * 128:
                                (2 * pair + half + 1) * 128], iff[:],
                           sem=(sm_t if (half == 1 and c == NCH - 1)
                                else None))
                k.wait("v", sm_t)
                for half in range(2):
                    for c in range(NCH):
                        off = (half * 4 + (c // 2)) * 512 + (c % 2) * 128
                        vop("tensor_copy",
                            sem=(sm_v if (half == 1 and c == NCH - 1)
                                 else None),
                            out=sb_[:, half * 1024 + c * 128:
                                    half * 1024 + c * 128 + 128],
                            in_=ps[:, off:off + 128])
                emarks.append(sm_v.v)
                k.wait("y", sm_v)
                k.dma("y", hout[(2 * pair) * 128:(2 * pair + 1) * 128, :],
                      sb_[:, 0:1024], sm_st)
                k.dma("y", hout[(2 * pair + 1) * 128:(2 * pair + 2) * 128, :],
                      sb_[:, 1024:2048], sm_st)
                store_marks.append(sm_st.v)
            k.wait("y", sm_st)
        scx.close()

        @blk.sync
        def _(e):
            with nc.allow_low_precision(reason="bf16 staging of matmul operands"):
                for fn in k.q["y"]:
                    fn(e)

        @blk.gpsimd
        def _(e):
            with nc.allow_low_precision(reason="bf16 staging of matmul operands"):
                for fn in k.q["g"]:
                    fn(e)

        @blk.vector
        def _(e):
            with nc.allow_low_precision(reason="bf16 staging of matmul operands"):
                for fn in k.q["v"]:
                    fn(e)

        @blk.scalar
        def _(e):
            with nc.allow_low_precision(reason="bf16 staging of matmul operands"):
                for fn in k.q["s"]:
                    fn(e)

        @blk.tensor
        def _(e):
            with nc.allow_low_precision(reason="bf16 staging of matmul operands"):
                for fn in k.q["t"]:
                    fn(e)

    return nc


def kernel(**inputs):
    xfull = np.ascontiguousarray(inputs["L0_attn_x"], dtype=np.float32)
    nc = build_nc()
    in_maps = []
    for i in range(8):
        m = {"x_in": np.ascontiguousarray(xfull[i])}
        for nm, v in inputs.items():
            if nm == "L0_attn_x":
                continue
            m[nm] = np.ascontiguousarray(v)
        in_maps.append(m)
    res = run_bass_kernel_spmd(nc, in_maps, core_ids=list(range(8)))
    out = np.stack([np.asarray(r["out"]) for r in res.results], axis=0)
    return out.astype(np.float32)
